# version 05 (final)
# baseline (speedup 1.0000x reference)
import contextlib
import math
import numpy as np
import ml_dtypes
import concourse.bass as bass
import concourse.mybir as mybir
from concourse.bass_utils import run_bass_kernel_spmd

F32 = mybir.dt.float32
BF16 = mybir.dt.bfloat16
U8 = mybir.dt.uint8
AF = mybir.ActivationFunctionType
ALU = mybir.AluOpType

B, S, D, H, DH, FF, R = 2, 4096, 2048, 16, 128, 8192, 32
NF = D // 128
TC = 1024
NT = S // 512
EPS_LN = 1e-5
EPS_RMS = float(np.finfo(np.float32).eps)
C_FFN = 32.0
C_MIX = 64.0

ENGS = ("tensor", "vector", "scalar", "gpsimd", "sync")
SEM_OWNER = {"dS": "sync", "dG": "gpsimd", "dA": "scalar",
             "pe": "tensor", "dv": "vector", "ac": "scalar", "cc": "gpsimd"}


class Ctx:
    def __init__(self, nc, block, sems):
        self.nc, self.block, self.S = nc, block, sems
        self.v = {k: 0 for k in sems}

    def inc(self, k, amt=1):
        self.v[k] += amt
        return self.v[k]

    def sec(self, eng, fn):
        getattr(self.block, eng)(fn)

    def barrier(self):
        snap = {k: v for k, v in self.v.items() if v > 0}
        for eng in ENGS:
            def fn(e, snap=snap, eng=eng):
                for k, val in snap.items():
                    if SEM_OWNER[k] != eng:
                        e.wait_ge(self.S[k], val)
            self.sec(eng, fn)


def pack_lhsT(wt):
    K, O = wt.shape
    a = wt.reshape(K // 128, 128, O // 128, 128).transpose(2, 1, 0, 3)
    return np.ascontiguousarray(a.reshape(O // 128, 128, (K // 128) * 128))


def pack_vec(v):
    return np.ascontiguousarray(np.asarray(v, np.float32).reshape(-1, 128).T)


def build(upto="full"):
    nc = bass.Bass("TRN2", target_bir_lowering=False)

    _uid = [0]
    _pt, _st = nc.psum_tensor, nc.sbuf_tensor

    def _psum_u(name, shape, dtype):
        _uid[0] += 1
        return _pt("%s_u%d" % (name, _uid[0]), shape, dtype)

    def _sbuf_u(name, shape, dtype):
        _uid[0] += 1
        return _st("%s_u%d" % (name, _uid[0]), shape, dtype)

    nc.psum_tensor = _psum_u
    nc.sbuf_tensor = _sbuf_u

    def din(name, shape, dtype=F32):
        return nc.dram_tensor(name, list(shape), dtype, kind="ExternalInput").ap()

    xin = din("xpack", [128, NF * TC])
    wqph = din("wqph", [4, NF, 128, 128])
    wkph = din("wkph", [4, NF, 128, 128])
    wvph = din("wvph", [NF, 128, 512])
    wmph = din("wmph", [4, NF, 128, 128])
    wophT = din("wophT", [4, NF, 128, 128])
    wWp = din("wWp", [4, NF, 128, 128])
    wf1p = din("wf1p", [FF // 128, 128, D])
    wf2p = din("wf2p", [NF, 128, FF])
    wdnp = din("wdnp", [128, NF * R])
    wupp = din("wupp", [R, D])
    wrtp = din("wrtp", [128, NF * 3])
    g1v = din("g1v", [128, 1]); g2v = din("g2v", [128, 1]); g3v = din("g3v", [128, 1])
    sbv = din("sbv", [128, NF])
    l1s = din("l1s", [128, NF]); l1b = din("l1b", [128, NF])
    l2s = din("l2s", [128, NF]); l2b = din("l2b", [128, NF])
    idbf = din("idbf", [128, 128], BF16)
    onescr = din("onescr", [128, 264], BF16)
    epscst = din("epscst", [128, 8], F32)
    dmskin = din("dmskin", [128, 4 * 512], BF16)

    xout = nc.dram_tensor("xout", [128, NF * TC], F32, kind="ExternalOutput").ap()

    def dint(name, shape, dtype=BF16):
        return nc.dram_tensor(name, list(shape), dtype).ap()

    hsp = dint("hsp", [128, NF * TC])
    xisp = dint("xisp", [128, NF * TC])
    agx_in = dint("agx_in", [128, NF * TC]); agx_out = dint("agx_out", [512, NF * TC])
    agm_in = dint("agm_in", [128, NF * TC]); agm_out = dint("agm_out", [512, NF * TC])
    agq = dint("agq", [128, 4 * S])
    agr = dint("agr", [128, 4 * S])
    agc = dint("agc", [128, 4 * S])
    agy = dint("agy", [128, 4 * S])
    rs_in = dint("rs_in", [64 * 128, TC], F32)
    rs_out = dint("rs_out", [NF * 128, TC], F32)
    rs2_in = dint("rs2_in", [64 * 128, TC], F32)
    rs2_out = dint("rs2_out", [NF * 128, TC], F32)
    RG = [[0, 1, 2, 3], [4, 5, 6, 7]]

    with contextlib.ExitStack() as stack:
        def sb(name, shape, dtype, st=stack):
            return st.enter_context(nc.sbuf_tensor(name, list(shape), dtype))

        XIX = sb("XIX", [128, NF * TC], F32)
        CONS = sb("CONS", [128, 264], BF16)
        DMSK = sb("DMSK", [128, 4 * 512], BF16)
        VEC = sb("VEC", [128, 128], F32)
        MSKB = sb("MSKB", [128, 2 * TC], U8)
        ZLR = sb("ZLR", [R, TC], BF16)
        WUPS = sb("WUPS", [R, D], BF16)
        WDN = sb("WDN", [128, NF * R], BF16)
        WRT = sb("WRT", [128, NF * 3], BF16)
        SHR = sb("SHR", [128, 2 * TC], F32)
        SCA = sb("SCA", [128, TC], BF16)
        SCB = sb("SCB", [128, TC], BF16)
        ROWSB = sb("ROWSB", [1, 2 * TC], BF16)
        SM4 = sb("SM4", [32, 256], BF16)

        s_dS = stack.enter_context(nc.semaphore("dS"))
        s_dG = stack.enter_context(nc.semaphore("dG"))
        s_dA = stack.enter_context(nc.semaphore("dA"))
        s_pe = stack.enter_context(nc.semaphore("pe"))
        s_dv = stack.enter_context(nc.semaphore("dv"))
        s_ac = stack.enter_context(nc.semaphore("ac"))
        s_cc = stack.enter_context(nc.semaphore("cc"))
        block = stack.enter_context(nc.Block())
        sems = {"dS": s_dS, "dG": s_dG, "dA": s_dA, "pe": s_pe,
                "dv": s_dv, "ac": s_ac, "cc": s_cc}
        c = Ctx(nc, block, sems)

        ID_BF = CONS[:, 0:128]
        ONESC = CONS[:, 128:129]
        ONESR = CONS[0:1, 136:264]
        G1 = VEC[:, 0:1]; G2 = VEC[:, 1:2]; G3 = VEC[:, 2:3]; GG = VEC[:, 3:4]
        EPSL = VEC[:, 4:5]; EPSR = VEC[:, 5:6]; INV128 = VEC[:, 6:7]
        SQD = VEC[:, 7:8]; CFN = VEC[:, 8:9]; CMX = VEC[:, 9:10]
        SBVv = VEC[:, 16:32]
        L1Sv = VEC[:, 32:48]; L1Bv = VEC[:, 48:64]
        L2Sv = VEC[:, 64:80]; L2Bv = VEC[:, 80:96]
        PST = VEC[:, 96:98]
        MUr = ROWSB[0:1, 0:TC]
        RSr = ROWSB[0:1, TC:2 * TC]

        def p0s(e):
            e.dma_start(out=XIX[:, :], in_=xin[:, :]).then_inc(s_dS, 16)
            e.dma_start(out=CONS[:, :], in_=onescr[:, :]).then_inc(s_dS, 16)
            e.dma_start(out=DMSK[:, :], in_=dmskin[:, :]).then_inc(s_dS, 16)
            e.dma_start(out=G1, in_=g1v[:, :]).then_inc(s_dS, 16)
            e.dma_start(out=G2, in_=g2v[:, :]).then_inc(s_dS, 16)
            e.dma_start(out=G3, in_=g3v[:, :]).then_inc(s_dS, 16)
            e.dma_start(out=VEC[:, 4:12], in_=epscst[:, :]).then_inc(s_dS, 16)
            e.dma_start(out=SBVv, in_=sbv[:, :]).then_inc(s_dS, 16)
            e.dma_start(out=L1Sv, in_=l1s[:, :]).then_inc(s_dS, 16)
            e.dma_start(out=L1Bv, in_=l1b[:, :]).then_inc(s_dS, 16)
            e.dma_start(out=L2Sv, in_=l2s[:, :]).then_inc(s_dS, 16)
            e.dma_start(out=L2Bv, in_=l2b[:, :]).then_inc(s_dS, 16)
        c.sec("sync", p0s)
        c.inc("dS", 12 * 16)

        def p0g(e):
            e.dma_start(out=WDN[:, :], in_=wdnp[:, :]).then_inc(s_dG, 16)
            e.dma_start(out=WUPS[:, :], in_=wupp[:, :]).then_inc(s_dG, 16)
            e.dma_start(out=WRT[:, :], in_=wrtp[:, :]).then_inc(s_dG, 16)
        c.sec("gpsimd", p0g)
        c.inc("dG", 3 * 16)

        def p0v(e, w=c.v["dS"]):
            e.wait_ge(s_dS, w)
            e.tensor_mul(GG, G1, G2)
            e.tensor_copy(VEC[:, 126:127], G1)
            e.tensor_copy(VEC[:, 127:128], G2)
            e.tensor_scalar_mul(GG, GG, 1.0 / math.sqrt(DH)).then_inc(s_dv, 1)
        c.sec("vector", p0v)
        c.inc("dv")
        c.barrier()

        def dump(name, ap, dtype):
            dt = nc.dram_tensor(name, list(ap.shape), dtype, kind="ExternalOutput").ap()
            c.barrier()

            def fn(e, dt=dt, ap=ap):
                e.dma_start(out=dt[:, :], in_=ap).then_inc(s_dS, 16)
            c.sec("sync", fn)
            c.inc("dS", 16)

        def dump_dram(name, dsrc, cols):
            tmp = stack.enter_context(nc.sbuf_tensor("dbgt", [128, cols], BF16))
            c.barrier()

            def fn(e, tmp=tmp, dsrc=dsrc, cols=cols):
                e.dma_start(out=tmp[:, :], in_=dsrc[:, 0:cols]).then_inc(s_dS, 16)
            c.sec("sync", fn)
            c.inc("dS", 16)
            dump(name, tmp[:, :], BF16)

        def finish():
            snap = dict(c.v)

            def fn(e, snap=snap):
                for k, val in snap.items():
                    if val > 0 and SEM_OWNER[k] != "sync":
                        e.wait_ge(sems[k], val)
                e.dma_start(out=xout[:, :], in_=XIX[:, :]).then_inc(s_dS, 16)
                e.wait_ge(s_dS, snap["dS"] + 16)
            c.sec("sync", fn)
            c.inc("dS", 16)

        def bcast_rows(pm, pr):
            def pe_fn(t, w=dict(c.v)):
                t.wait_ge(s_dv, w["dv"])
                for hh in range(2):
                    t.matmul(pm[:, hh * 512:hh * 512 + 512], ONESR,
                             MUr[:, hh * 512:hh * 512 + 512], start=True, stop=True)
                    ins = t.matmul(pr[:, hh * 512:hh * 512 + 512], ONESR,
                                   RSr[:, hh * 512:hh * 512 + 512],
                                   start=True, stop=True)
                ins.then_inc(s_pe, 1)
            c.sec("tensor", pe_fn)
            return c.inc("pe")

        def ln_stats(src_bf, pa, pb, eps_ap):
            pe0 = c.v["pe"]
            for f in range(NF):
                def ac_fn(e, f=f, w=pe0 + f):
                    if f >= 1:
                        e.wait_ge(s_pe, w)
                    e.activation(SCB[:, :], src_bf(f), AF.Square).then_inc(s_ac, 1)
                c.sec("scalar", ac_fn)
                ac_v = c.inc("ac")

                def pe_fn(t, f=f, ac_v=ac_v):
                    t.wait_ge(s_ac, ac_v)
                    for hh in range(2):
                        t.matmul(pa[0:1, hh * 512:hh * 512 + 512], ONESC,
                                 src_bf(f)[:, hh * 512:hh * 512 + 512],
                                 start=(f == 0), stop=(f == NF - 1))
                        ins = t.matmul(pb[0:1, hh * 512:hh * 512 + 512], ONESC,
                                       SCB[:, hh * 512:hh * 512 + 512],
                                       start=(f == 0), stop=(f == NF - 1))
                    ins.then_inc(s_pe, 1)
                c.sec("tensor", pe_fn)
                c.inc("pe")

            MUf = SHR[0:1, 0:TC]; VARf = SHR[0:1, TC:2 * TC]

            def dv_fn(e, w=c.v["pe"]):
                e.wait_ge(s_pe, w)
                e.tensor_scalar_mul(MUf, pa[0:1, 0:TC], 1.0 / D)
                e.tensor_scalar_mul(VARf, pb[0:1, 0:TC], 1.0 / D)
                e.tensor_mul(pa[0:1, 0:TC], MUf, MUf)
                e.tensor_sub(VARf, VARf, pa[0:1, 0:TC]).then_inc(s_dv, 1)
            c.sec("vector", dv_fn)
            dv_v = c.inc("dv")

            def ac2(e, dv_v=dv_v, eps_ap=eps_ap):
                e.wait_ge(s_dv, dv_v)
                e.activation(VARf, VARf, AF.Sqrt, bias=eps_ap).then_inc(s_ac, 1)
            c.sec("scalar", ac2)
            ac_v = c.inc("ac")

            def dv2(e, ac_v=ac_v):
                e.wait_ge(s_ac, ac_v)
                e.reciprocal(VARf, VARf)
                e.tensor_copy(MUr, MUf)
                e.tensor_copy(RSr, VARf).then_inc(s_dv, 1)
            c.sec("vector", dv2)
            c.inc("dv")

        def emit_ln_mac(HBF):
            with contextlib.ExitStack() as ps:
                pa = ps.enter_context(nc.psum_tensor("lma", [128, TC], F32))
                pb = ps.enter_context(nc.psum_tensor("lmb", [128, TC], F32))
                pe0 = c.v["pe"]
                for f in range(NF):
                    def ac_fn(e, f=f, w=pe0 + f):
                        if f >= 1:
                            e.wait_ge(s_pe, w)
                        e.activation(SCA[:, :], XIX[:, f * TC:(f + 1) * TC], AF.Copy)
                        e.activation(SCB[:, :], XIX[:, f * TC:(f + 1) * TC],
                                     AF.Square).then_inc(s_ac, 1)
                    c.sec("scalar", ac_fn)
                    ac_v = c.inc("ac")

                    def pe_fn(t, f=f, ac_v=ac_v):
                        t.wait_ge(s_ac, ac_v)
                        for hh in range(2):
                            t.matmul(pa[0:1, hh * 512:hh * 512 + 512], ONESC,
                                     SCA[:, hh * 512:hh * 512 + 512],
                                     start=(f == 0), stop=(f == NF - 1))
                            ins = t.matmul(pb[0:1, hh * 512:hh * 512 + 512], ONESC,
                                           SCB[:, hh * 512:hh * 512 + 512],
                                           start=(f == 0), stop=(f == NF - 1))
                        ins.then_inc(s_pe, 1)
                    c.sec("tensor", pe_fn)
                    c.inc("pe")

                MUf = SHR[0:1, 0:TC]; VARf = SHR[0:1, TC:2 * TC]

                def dv_fn(e, w=c.v["pe"]):
                    e.wait_ge(s_pe, w)
                    e.tensor_scalar_mul(MUf, pa[0:1, 0:TC], 1.0 / D)
                    e.tensor_scalar_mul(VARf, pb[0:1, 0:TC], 1.0 / D)
                    e.tensor_mul(pa[0:1, 0:TC], MUf, MUf)
                    e.tensor_sub(VARf, VARf, pa[0:1, 0:TC]).then_inc(s_dv, 1)
                c.sec("vector", dv_fn)
                dv_v = c.inc("dv")

                def ac2(e, dv_v=dv_v):
                    e.wait_ge(s_dv, dv_v)
                    e.activation(VARf, VARf, AF.Sqrt, bias=EPSL[0:1, 0:1]).then_inc(s_ac, 1)
                c.sec("scalar", ac2)
                ac_v = c.inc("ac")

                def dv2(e, ac_v=ac_v):
                    e.wait_ge(s_ac, ac_v)
                    e.reciprocal(VARf, VARf)
                    e.tensor_copy(MUr, MUf)
                    e.tensor_copy(RSr, VARf).then_inc(s_dv, 1)
                c.sec("vector", dv2)
                c.inc("dv")

                rb = bcast_rows(pa, pb)

                def dv3(e, rb=rb, HBF=HBF):
                    e.wait_ge(s_pe, rb)
                    for f in range(NF):
                        e.tensor_sub(SHR[:, 0:TC], XIX[:, f * TC:(f + 1) * TC], pa[:, :])
                        e.tensor_mul(HBF[:, f * TC:(f + 1) * TC], SHR[:, 0:TC], pb[:, :])
                    e.tensor_copy(SHR[0:1, 0:1], pa[0:1, 0:1]).then_inc(s_dv, 1)
                c.sec("vector", dv3)
                c.inc("dv")
            c.barrier()

        def emit_ln_aff(src_bf, sc_ap, sb_ap, dst_bf):
            with contextlib.ExitStack() as ps:
                pa = ps.enter_context(nc.psum_tensor("lia", [128, TC], F32))
                pb = ps.enter_context(nc.psum_tensor("lib", [128, TC], F32))
                ln_stats(src_bf, pa, pb, EPSL[0:1, 0:1])
                rb = bcast_rows(pa, pb)

                def dv3(e, rb=rb):
                    e.wait_ge(s_pe, rb)
                    for f in range(NF):
                        e.tensor_sub(SHR[:, 0:TC], src_bf(f), pa[:, :])
                        e.tensor_mul(SHR[:, TC:2 * TC], SHR[:, 0:TC], pb[:, :])
                        e.tensor_scalar(dst_bf(f), SHR[:, TC:2 * TC],
                                        sc_ap[:, f:f + 1], sb_ap[:, f:f + 1],
                                        ALU.mult, ALU.add)
                    e.tensor_copy(SHR[0:1, 0:1], SHR[0:1, 0:1]).then_inc(s_dv, 1)
                c.sec("vector", dv3)
                c.inc("dv")
            c.barrier()

        def emit_router(HBF):
            with contextlib.ExitStack() as ps:
                pa = ps.enter_context(nc.psum_tensor("rta", [128, TC], F32))
                pb = ps.enter_context(nc.psum_tensor("rtb", [128, TC], F32))
                pt = ps.enter_context(nc.psum_tensor("rtt", [128, 1024], BF16))

                def pe_rt(t, w=dict(c.v)):
                    t.wait_ge(s_dv, w["dv"])
                    t.wait_ge(s_dG, w["dG"])
                    for tt in range(8):
                        for f in range(NF):
                            t.matmul(pa[:, tt * 3:tt * 3 + 3],
                                     HBF[:, f * TC + tt * 128:f * TC + (tt + 1) * 128],
                                     WRT[:, f * 3:(f + 1) * 3],
                                     start=(f == 0), stop=(f == NF - 1))
                    for f in range(NF):
                        for hh in range(2):
                            ins = t.matmul(pb[0:R, hh * 512:hh * 512 + 512],
                                           WDN[:, f * R:(f + 1) * R],
                                           HBF[:, f * TC + hh * 512:
                                               f * TC + hh * 512 + 512],
                                           start=(f == 0), stop=(f == NF - 1))
                    ins.then_inc(s_pe, 1)
                c.sec("tensor", pe_rt)
                pe_v = c.inc("pe")

                def dv_rt(e, pe_v=pe_v):
                    e.wait_ge(s_pe, pe_v)
                    e.tensor_copy(SHR[:, 64:88], pa[:, 0:24])
                    e.tensor_copy(ZLR[:, :], pb[0:R, :])
                    A01 = lambda tt: SHR[:, tt:tt + 1]
                    A02 = lambda tt: SHR[:, 8 + tt:9 + tt]
                    A12 = lambda tt: SHR[:, 16 + tt:17 + tt]
                    M0 = lambda tt: SHR[:, 24 + tt:25 + tt]
                    T4 = lambda tt: SHR[:, 32 + tt:33 + tt]
                    M1 = lambda tt: SHR[:, 40 + tt:41 + tt]
                    for tt in range(8):
                        l0 = SHR[:, 64 + tt * 3:64 + tt * 3 + 1]
                        l1 = SHR[:, 64 + tt * 3 + 1:64 + tt * 3 + 2]
                        l2 = SHR[:, 64 + tt * 3 + 2:64 + tt * 3 + 3]
                        e.tensor_tensor(A01(tt), l0, l1, ALU.is_ge)
                        e.tensor_tensor(A02(tt), l0, l2, ALU.is_ge)
                        e.tensor_tensor(A12(tt), l1, l2, ALU.is_ge)
                    for tt in range(8):
                        e.tensor_mul(M0(tt), A01(tt), A02(tt))
                    for tt in range(8):
                        e.tensor_scalar(T4(tt), M0(tt), -1.0, 1.0, ALU.mult, ALU.add)
                    for tt in range(8):
                        e.tensor_mul(M1(tt), T4(tt), A12(tt))
                    for tt in range(8):
                        e.tensor_copy(SCA[:, tt:tt + 1], M0(tt))
                        e.tensor_copy(SCA[:, 8 + tt:8 + tt + 1], M1(tt))
                    e.tensor_copy(SCA[0:1, 16:17], SCA[0:1, 0:1]).then_inc(s_dv, 1)
                c.sec("vector", dv_rt)
                dv_v = c.inc("dv")

                def pe_tp(t, dv_v=dv_v):
                    t.wait_ge(s_dv, dv_v)
                    t.transpose(pt[0:16, 0:128], SCA[:, 0:16],
                                ID_BF).then_inc(s_pe, 1)
                c.sec("tensor", pe_tp)
                pe_v = c.inc("pe")

                def dv_tp(e, pe_v=pe_v):
                    e.wait_ge(s_pe, pe_v)
                    e.tensor_copy(SM4[0:16, 0:128], pt[0:16, 0:128]).then_inc(s_dv, 1)
                c.sec("vector", dv_tp)
                dv_v = c.inc("dv")

                def mvr(e, dv_v=dv_v):
                    e.wait_ge(s_dv, dv_v)
                    for tt in range(8):
                        e.dma_start(out=ROWSB[0:1, tt * 128:(tt + 1) * 128],
                                    in_=SM4[tt:tt + 1, 0:128]).then_inc(s_dS, 16)
                        e.dma_start(out=ROWSB[0:1, TC + tt * 128:TC + (tt + 1) * 128],
                                    in_=SM4[8 + tt:8 + tt + 1, 0:128]).then_inc(s_dS, 16)
                c.sec("sync", mvr)
                dS_v = c.inc("dS", 16 * 16)

                def pe_bc(t, dS_v=dS_v):
                    t.wait_ge(s_dS, dS_v)
                    for hh in range(2):
                        t.matmul(pa[:, hh * 512:hh * 512 + 512], ONESR,
                                 MUr[:, hh * 512:hh * 512 + 512],
                                 start=True, stop=True)
                        ins = t.matmul(pb[:, hh * 512:hh * 512 + 512], ONESR,
                                       RSr[:, hh * 512:hh * 512 + 512],
                                       start=True, stop=True)
                    ins.then_inc(s_pe, 1)
                c.sec("tensor", pe_bc)
                pe_v = c.inc("pe")

                def dv_bc(e, pe_v=pe_v):
                    e.wait_ge(s_pe, pe_v)
                    e.tensor_copy(MSKB[:, 0:TC], pa[:, :])
                    e.tensor_copy(MSKB[:, TC:2 * TC], pb[:, :]).then_inc(s_dv, 1)
                c.sec("vector", dv_bc)
                c.inc("dv")
            c.barrier()

        def emit_gather(src_sbuf, ag_in, ag_out):
            def st(e):
                e.dma_start(out=ag_in[:, :], in_=src_sbuf[:, :]).then_inc(s_dA, 16)
            c.sec("scalar", st)
            da = c.inc("dA", 16)

            def cg(e, da=da):
                e.wait_ge(s_dA, da)
                e.collective_compute("AllGather", ALU.bypass, replica_groups=RG,
                                     ins=[ag_in.tensor.ap().opt()],
                                     outs=[ag_out.tensor.ap().opt()]).then_inc(s_cc, 1)
            c.sec("gpsimd", cg)
            c.inc("cc")
            c.barrier()

        def emit_qkv(KF, VT):
            with contextlib.ExitStack() as ps:
                WVS = ps.enter_context(nc.sbuf_tensor("WVS", [128, NF * 512], BF16))
                XG = ps.enter_context(nc.sbuf_tensor("XG", [128, NF * 512], BF16))
                pv0 = ps.enter_context(nc.psum_tensor("pv0", [128, 512], F32))
                pv1 = ps.enter_context(nc.psum_tensor("pv1", [128, 512], F32))
                pvs = [pv0, pv1]

                def wld(e):
                    for f in range(NF):
                        e.dma_start(out=WVS[:, f * 512:(f + 1) * 512],
                                    in_=wvph[f, :, :]).then_inc(s_dG, 16)
                c.sec("gpsimd", wld)
                wv_v = c.inc("dG", NF * 16)

                for tb in range(NT):
                    def xg(e, tb=tb, w=c.v["pe"]):
                        if tb > 0:
                            e.wait_ge(s_pe, w)
                        r0 = 128 * (tb // 2)
                        cs = (tb % 2) * 512
                        for f in range(NF):
                            e.dma_start(out=XG[:, f * 512:(f + 1) * 512],
                                        in_=agx_out[r0:r0 + 128,
                                                    f * TC + cs:f * TC + cs + 512]
                                        ).then_inc(s_dS, 16)
                    c.sec("sync", xg)
                    xg_v = c.inc("dS", NF * 16)

                    for ts in range(4):
                        par = (tb * 4 + ts) % 2

                        def pe_fn(t, tb=tb, ts=ts, par=par, xg_v=xg_v, wv_v=wv_v,
                                  w=c.v["ac"] - 1):
                            t.wait_ge(s_dS, xg_v)
                            if tb == 0 and ts == 0:
                                t.wait_ge(s_dG, wv_v)
                            if w > 0:
                                t.wait_ge(s_ac, w)
                            for f in range(NF):
                                ins = t.matmul(pvs[par][:, :],
                                               XG[:, f * 512 + ts * 128:
                                                  f * 512 + (ts + 1) * 128],
                                               WVS[:, f * 512:(f + 1) * 512],
                                               start=(f == 0), stop=(f == NF - 1))
                            ins.then_inc(s_pe, 1)
                        c.sec("tensor", pe_fn)
                        pe_v = c.inc("pe")

                        def ac_fn(e, tb=tb, ts=ts, par=par, pe_v=pe_v):
                            e.wait_ge(s_pe, pe_v)
                            e.activation(VT[:, (tb * 4 + ts) * 512:(tb * 4 + ts + 1) * 512],
                                         pvs[par][:, :], AF.Copy).then_inc(s_ac, 1)
                        c.sec("scalar", ac_fn)
                        c.inc("ac")
            c.barrier()

            with contextlib.ExitStack() as ps:
                WQS = ps.enter_context(nc.sbuf_tensor("WQS", [128, 4 * D], BF16))
                XG = ps.enter_context(nc.sbuf_tensor("XG2", [128, NF * 512], BF16))
                QTB = ps.enter_context(nc.sbuf_tensor("QTB", [128, 2 * 512], BF16))
                pq0 = ps.enter_context(nc.psum_tensor("pq0", [128, 512], F32))

                def wld(e):
                    for hl in range(4):
                        for f in range(NF):
                            e.dma_start(out=WQS[:, hl * D + f * 128:hl * D + (f + 1) * 128],
                                        in_=wqph[hl, f, :, :]).then_inc(s_dG, 16)
                c.sec("gpsimd", wld)
                wq_v = c.inc("dG", 4 * NF * 16)

                for tb in range(NT):
                    def xg(e, tb=tb, w=c.v["pe"]):
                        if tb > 0:
                            e.wait_ge(s_pe, w)
                        r0 = 128 * (tb // 2)
                        cs = (tb % 2) * 512
                        for f in range(NF):
                            e.dma_start(out=XG[:, f * 512:(f + 1) * 512],
                                        in_=agx_out[r0:r0 + 128,
                                                    f * TC + cs:f * TC + cs + 512]
                                        ).then_inc(s_dS, 16)
                    c.sec("sync", xg)
                    xg_v = c.inc("dS", NF * 16)

                    for hl in range(4):
                        par = (tb * 4 + hl) % 2

                        def pe_q(t, tb=tb, hl=hl, xg_v=xg_v, wq_v=wq_v,
                                 wac=c.v["ac"], wda=c.v["dA"] - 16):
                            t.wait_ge(s_dS, xg_v)
                            if tb == 0 and hl == 0:
                                t.wait_ge(s_dG, wq_v)
                            if wac > 0:
                                t.wait_ge(s_ac, wac)
                            if wda > 0:
                                t.wait_ge(s_dA, wda)
                            for f in range(NF):
                                ins = t.matmul(pq0[:, :],
                                               WQS[:, hl * D + f * 128:
                                                  hl * D + (f + 1) * 128],
                                               XG[:, f * 512:(f + 1) * 512],
                                               start=(f == 0), stop=(f == NF - 1))
                            ins.then_inc(s_pe, 1)
                        c.sec("tensor", pe_q)
                        pe_v = c.inc("pe")

                        def ac_q(e, par=par, pe_v=pe_v):
                            e.wait_ge(s_pe, pe_v)
                            e.activation(QTB[:, par * 512:(par + 1) * 512],
                                         pq0[:, :], AF.Copy,
                                         scale=SQD).then_inc(s_ac, 1)
                        c.sec("scalar", ac_q)
                        ac_v = c.inc("ac")

                        def stq(e, tb=tb, hl=hl, par=par, ac_v=ac_v):
                            e.wait_ge(s_ac, ac_v)
                            e.dma_start(out=agq[:, hl * S + tb * 512:hl * S + (tb + 1) * 512],
                                        in_=QTB[:, par * 512:(par + 1) * 512]
                                        ).then_inc(s_dA, 16)
                        c.sec("scalar", stq)
                        c.inc("dA", 16)
            c.barrier()

            with contextlib.ExitStack() as ps:
                WKS = ps.enter_context(nc.sbuf_tensor("WKS", [128, 4 * D], BF16))
                XG = ps.enter_context(nc.sbuf_tensor("XG3", [128, NF * 512], BF16))
                pq1 = ps.enter_context(nc.psum_tensor("pq1", [128, 512], F32))

                def wld(e):
                    for hl in range(4):
                        for f in range(NF):
                            e.dma_start(out=WKS[:, hl * D + f * 128:hl * D + (f + 1) * 128],
                                        in_=wkph[hl, f, :, :]).then_inc(s_dG, 16)
                c.sec("gpsimd", wld)
                wk_v = c.inc("dG", 4 * NF * 16)

                for tb in range(NT):
                    def xg(e, tb=tb, w=c.v["pe"]):
                        if tb > 0:
                            e.wait_ge(s_pe, w)
                        r0 = 128 * (tb // 2)
                        cs = (tb % 2) * 512
                        for f in range(NF):
                            e.dma_start(out=XG[:, f * 512:(f + 1) * 512],
                                        in_=agx_out[r0:r0 + 128,
                                                    f * TC + cs:f * TC + cs + 512]
                                        ).then_inc(s_dS, 16)
                    c.sec("sync", xg)
                    xg_v = c.inc("dS", NF * 16)

                    for hl in range(4):
                        def pe_k(t, tb=tb, hl=hl, xg_v=xg_v, wk_v=wk_v,
                                 w=c.v["dv"]):
                            t.wait_ge(s_dS, xg_v)
                            if tb == 0 and hl == 0:
                                t.wait_ge(s_dG, wk_v)
                            if w > 0:
                                t.wait_ge(s_dv, w)
                            for f in range(NF):
                                ins = t.matmul(pq1[:, :],
                                               WKS[:, hl * D + f * 128:
                                                  hl * D + (f + 1) * 128],
                                               XG[:, f * 512:(f + 1) * 512],
                                               start=(f == 0), stop=(f == NF - 1))
                            ins.then_inc(s_pe, 1)
                        c.sec("tensor", pe_k)
                        pe_v = c.inc("pe")

                        def dv_k(e, tb=tb, hl=hl, pe_v=pe_v):
                            e.wait_ge(s_pe, pe_v)
                            e.tensor_copy(KF[:, hl * S + tb * 512:hl * S + (tb + 1) * 512],
                                          pq1[:, :]).then_inc(s_dv, 1)
                        c.sec("vector", dv_k)
                        c.inc("dv")
            c.barrier()

        def emit_attn(KF, VT):
            with contextlib.ExitStack() as ps:
                QJ = ps.enter_context(nc.sbuf_tensor("QJ", [128, 2 * 512], BF16))
                ET = ps.enter_context(nc.sbuf_tensor("ET", [128, 2 * 512], BF16))
                CXT = ps.enter_context(nc.sbuf_tensor("CXT", [128, 2 * 512], BF16))
                pe0 = ps.enter_context(nc.psum_tensor("pae0", [128, 512], F32))
                pe1 = ps.enter_context(nc.psum_tensor("pae1", [128, 512], F32))
                py = ps.enter_context(nc.psum_tensor("pay", [128, 512], F32))
                pl = ps.enter_context(nc.psum_tensor("pal", [1, 512], F32))
                pB = ps.enter_context(nc.psum_tensor("paB", [128, 512], F32))
                pes = [pe0, pe1]

                for hl in range(4):
                    for j in range(NT):
                        jpar = (hl * NT + j) % 2

                        def qld(e, hl=hl, j=j, jpar=jpar, w=c.v["pe"]):
                            if hl + j > 0:
                                e.wait_ge(s_pe, w)
                            e.dma_start(out=QJ[:, jpar * 512:(jpar + 1) * 512],
                                        in_=agq[:, hl * S + j * 512:hl * S + (j + 1) * 512]
                                        ).then_inc(s_dS, 16)
                        c.sec("sync", qld)
                        q_v = c.inc("dS", 16)

                        nsub = 4 * (j + 1)
                        for si in range(nsub):
                            g, s = si // 4, si % 4
                            par = si % 2

                            def pe_sc(t, hl=hl, g=g, s=s, par=par, q_v=q_v,
                                      jpar=jpar, si=si, w=c.v["ac"] - 1):
                                if si == 0:
                                    t.wait_ge(s_dS, q_v)
                                if w > 0:
                                    t.wait_ge(s_ac, w)
                                t.matmul(pes[par][:, :],
                                         KF[:, hl * S + (g * 4 + s) * 128:
                                            hl * S + (g * 4 + s + 1) * 128],
                                         QJ[:, jpar * 512:(jpar + 1) * 512],
                                         start=True, stop=True).then_inc(s_pe, 1)
                            c.sec("tensor", pe_sc)
                            sc_v = c.inc("pe")

                            diag = (g == j)

                            def ac_e(e, par=par, sc_v=sc_v, w=c.v["pe"] - 2):
                                e.wait_ge(s_pe, sc_v)
                                if w > 0:
                                    e.wait_ge(s_pe, w)
                                e.activation(ET[:, par * 512:(par + 1) * 512],
                                             pes[par][:, :], AF.Exp,
                                             bias=CFN).then_inc(s_ac, 1)
                            c.sec("scalar", ac_e)
                            e_v = c.inc("ac")

                            if diag:
                                def dv_m(e, par=par, s=s, e_v=e_v):
                                    e.wait_ge(s_ac, e_v)
                                    e.tensor_mul(ET[:, par * 512:(par + 1) * 512],
                                                 ET[:, par * 512:(par + 1) * 512],
                                                 DMSK[:, s * 512:(s + 1) * 512]
                                                 ).then_inc(s_dv, 1)
                                c.sec("vector", dv_m)
                                gate = (s_dv, c.inc("dv"))
                            else:
                                gate = (s_ac, e_v)

                            def pe_lc(t, hl=hl, g=g, s=s, par=par, si=si,
                                      nsub=nsub, gate=gate, w0=c.v["dv"]):
                                t.wait_ge(gate[0], gate[1])
                                if si == 0 and w0 > 0:
                                    t.wait_ge(s_dv, w0)
                                t.matmul(pl[0:1, :], ONESC,
                                         ET[:, par * 512:(par + 1) * 512],
                                         start=(si == 0), stop=(si == nsub - 1))
                                t.matmul(py[:, :],
                                         VT[:, (g * 4 + s) * 512 + hl * 128:
                                            (g * 4 + s) * 512 + (hl + 1) * 128],
                                         ET[:, par * 512:(par + 1) * 512],
                                         start=(si == 0),
                                         stop=(si == nsub - 1)).then_inc(s_pe, 1)
                            c.sec("tensor", pe_lc)
                            c.inc("pe")

                        def dv_f(e, w=c.v["pe"]):
                            e.wait_ge(s_pe, w)
                            e.reciprocal(SHR[0:1, 0:512], pl[0:1, :])
                            e.tensor_copy(ROWSB[0:1, 0:512],
                                          SHR[0:1, 0:512]).then_inc(s_dv, 1)
                        c.sec("vector", dv_f)
                        dv_v = c.inc("dv")

                        def pe_b(t, dv_v=dv_v):
                            t.wait_ge(s_dv, dv_v)
                            t.matmul(pB[:, :], ONESR, ROWSB[0:1, 0:512],
                                     start=True, stop=True).then_inc(s_pe, 1)
                        c.sec("tensor", pe_b)
                        pb_v = c.inc("pe")

                        def dv_c(e, jpar=jpar, pb_v=pb_v, w=c.v["dA"] - 16):
                            e.wait_ge(s_pe, pb_v)
                            if w > 0:
                                e.wait_ge(s_dA, w)
                            e.tensor_copy(SHR[:, 512:1024], py[:, :])
                            e.tensor_mul(CXT[:, jpar * 512:(jpar + 1) * 512],
                                         SHR[:, 512:1024], pB[:, :]
                                         ).then_inc(s_dv, 1)
                        c.sec("vector", dv_c)
                        dv_v = c.inc("dv")

                        def stc(e, hl=hl, j=j, jpar=jpar, dv_v=dv_v):
                            e.wait_ge(s_dv, dv_v)
                            e.dma_start(out=agc[:, hl * S + j * 512:hl * S + (j + 1) * 512],
                                        in_=CXT[:, jpar * 512:(jpar + 1) * 512]
                                        ).then_inc(s_dA, 16)
                        c.sec("scalar", stc)
                        c.inc("dA", 16)
            c.barrier()

        def emit_partial_rs(wsrc, actsrc, rin, rout, name):
            with contextlib.ExitStack() as ps:
                WOS = ps.enter_context(nc.sbuf_tensor("WOS" + name, [128, 4 * D], BF16))
                CTB = ps.enter_context(nc.sbuf_tensor("CTB" + name, [128, 2 * 4 * 512], BF16))
                pw0 = ps.enter_context(nc.psum_tensor("pw0" + name, [128, 512], F32))
                pw1 = ps.enter_context(nc.psum_tensor("pw1" + name, [128, 512], F32))
                pws = [pw0, pw1]

                def wld(e):
                    for hl in range(4):
                        for o in range(NF):
                            e.dma_start(out=WOS[:, hl * D + o * 128:hl * D + (o + 1) * 128],
                                        in_=wsrc[hl, o, :, :]).then_inc(s_dG, 16)
                c.sec("gpsimd", wld)
                w_v = c.inc("dG", 4 * NF * 16)

                for tb in range(NT):
                    tpar = tb % 2

                    def cld(e, tb=tb, tpar=tpar, w=c.v["pe"]):
                        if tb > 1:
                            e.wait_ge(s_pe, w)
                        for hl in range(4):
                            e.dma_start(out=CTB[:, (tpar * 4 + hl) * 512:
                                               (tpar * 4 + hl + 1) * 512],
                                        in_=actsrc[:, hl * S + tb * 512:
                                                   hl * S + (tb + 1) * 512]
                                        ).then_inc(s_dS, 16)
                    c.sec("sync", cld)
                    c_v = c.inc("dS", 4 * 16)

                    for o in range(NF):
                        par = (tb * NF + o) % 2

                        def pe_w(t, tb=tb, o=o, par=par, tpar=tpar, c_v=c_v,
                                 w_v=w_v, w=c.v["dv"] - 1):
                            t.wait_ge(s_dS, c_v)
                            if tb == 0 and o == 0:
                                t.wait_ge(s_dG, w_v)
                            if w > 0:
                                t.wait_ge(s_dv, w)
                            for hl in range(4):
                                ins = t.matmul(pws[par][:, :],
                                               WOS[:, hl * D + o * 128:
                                                  hl * D + (o + 1) * 128],
                                               CTB[:, (tpar * 4 + hl) * 512:
                                                   (tpar * 4 + hl + 1) * 512],
                                               start=(hl == 0), stop=(hl == 3))
                            ins.then_inc(s_pe, 1)
                        c.sec("tensor", pe_w)
                        pe_v = c.inc("pe")

                        def dv_e(e, par=par, pe_v=pe_v, w=c.v["dA"] - 16):
                            e.wait_ge(s_pe, pe_v)
                            if w > 0:
                                e.wait_ge(s_dA, w)
                            e.tensor_copy(SHR[:, par * 512:par * 512 + 512],
                                          pws[par][:, :]).then_inc(s_dv, 1)
                        c.sec("vector", dv_e)
                        dv_v = c.inc("dv")

                        def st_w(e, tb=tb, o=o, par=par, dv_v=dv_v):
                            e.wait_ge(s_dv, dv_v)
                            r0 = (tb // 2) * D + o * 128
                            cs = (tb % 2) * 512
                            e.dma_start(out=rin[r0:r0 + 128, cs:cs + 512],
                                        in_=SHR[:, par * 512:par * 512 + 512]
                                        ).then_inc(s_dA, 16)
                        c.sec("scalar", st_w)
                        c.inc("dA", 16)

                def crs(e, w=c.v["dA"]):
                    e.wait_ge(s_dA, w)
                    e.collective_compute("ReduceScatter", ALU.add, replica_groups=RG,
                                         ins=[rin.tensor.ap().opt()],
                                         outs=[rout.tensor.ap().opt()]).then_inc(s_cc, 1)
                c.sec("gpsimd", crs)
                c.inc("cc")
            c.barrier()

        def emit_xi():
            with contextlib.ExitStack() as ps:
                XT = ps.enter_context(nc.sbuf_tensor("XTxi", [128, 2 * TC], F32))
                HT = ps.enter_context(nc.sbuf_tensor("HTxi", [128, 2 * TC], BF16))
                XO = ps.enter_context(nc.sbuf_tensor("XOxi", [128, 2 * TC], BF16))
                for f in range(NF):
                    par = f % 2

                    def ld(e, f=f, par=par, w=c.v["dv"] - 1):
                        if w > 0:
                            e.wait_ge(s_dv, w)
                        e.dma_start(out=XT[:, par * TC:(par + 1) * TC],
                                    in_=rs_out[f * 128:(f + 1) * 128, :]).then_inc(s_dS, 16)
                        e.dma_start(out=HT[:, par * TC:(par + 1) * TC],
                                    in_=hsp[:, f * TC:(f + 1) * TC]).then_inc(s_dS, 16)
                    c.sec("sync", ld)
                    ld_v = c.inc("dS", 32)

                    def dv_a(e, f=f, par=par, ld_v=ld_v, w=c.v["dA"] - 16):
                        e.wait_ge(s_dS, ld_v)
                        if w > 0:
                            e.wait_ge(s_dA, w)
                        e.tensor_add(XO[:, par * TC:(par + 1) * TC],
                                     XT[:, par * TC:(par + 1) * TC],
                                     HT[:, par * TC:(par + 1) * TC]).then_inc(s_dv, 1)
                    c.sec("vector", dv_a)
                    dv_v = c.inc("dv")

                    def st(e, f=f, par=par, dv_v=dv_v):
                        e.wait_ge(s_dv, dv_v)
                        e.dma_start(out=xisp[:, f * TC:(f + 1) * TC],
                                    in_=XO[:, par * TC:(par + 1) * TC]).then_inc(s_dA, 16)
                    c.sec("scalar", st)
                    c.inc("dA", 16)
            c.barrier()

        def emit_ln2_stats(pa, pb):
            with contextlib.ExitStack() as ps:
                XT = ps.enter_context(nc.sbuf_tensor("XTl2", [128, 2 * TC], BF16))
                for f in range(NF):
                    par = f % 2

                    def ld(e, f=f, par=par, w=c.v["pe"] - 1):
                        if w > 0:
                            e.wait_ge(s_pe, w)
                        e.dma_start(out=XT[:, par * TC:(par + 1) * TC],
                                    in_=xisp[:, f * TC:(f + 1) * TC]).then_inc(s_dS, 16)
                    c.sec("sync", ld)
                    ld_v = c.inc("dS", 16)

                    def ac_s(e, par=par, ld_v=ld_v):
                        e.wait_ge(s_dS, ld_v)
                        e.activation(SCB[:, :], XT[:, par * TC:(par + 1) * TC],
                                     AF.Square).then_inc(s_ac, 1)
                    c.sec("scalar", ac_s)
                    ac_v = c.inc("ac")

                    def pe_fn(t, f=f, par=par, ac_v=ac_v):
                        t.wait_ge(s_ac, ac_v)
                        for hh in range(2):
                            t.matmul(pa[0:1, hh * 512:hh * 512 + 512], ONESC,
                                     XT[:, par * TC + hh * 512:
                                        par * TC + hh * 512 + 512],
                                     start=(f == 0), stop=(f == NF - 1))
                            ins = t.matmul(pb[0:1, hh * 512:hh * 512 + 512], ONESC,
                                           SCB[:, hh * 512:hh * 512 + 512],
                                           start=(f == 0), stop=(f == NF - 1))
                        ins.then_inc(s_pe, 1)
                    c.sec("tensor", pe_fn)
                    c.inc("pe")

                MUf = SHR[0:1, 0:TC]; VARf = SHR[0:1, TC:2 * TC]

                def dv_fn(e, w=c.v["pe"]):
                    e.wait_ge(s_pe, w)
                    e.tensor_scalar_mul(MUf, pa[0:1, 0:TC], 1.0 / D)
                    e.tensor_scalar_mul(VARf, pb[0:1, 0:TC], 1.0 / D)
                    e.tensor_mul(pa[0:1, 0:TC], MUf, MUf)
                    e.tensor_sub(VARf, VARf, pa[0:1, 0:TC]).then_inc(s_dv, 1)
                c.sec("vector", dv_fn)
                dv_v = c.inc("dv")

                def ac2(e, dv_v=dv_v):
                    e.wait_ge(s_dv, dv_v)
                    e.activation(VARf, VARf, AF.Sqrt, bias=EPSL[0:1, 0:1]).then_inc(s_ac, 1)
                c.sec("scalar", ac2)
                ac_v = c.inc("ac")

                def dv2(e, ac_v=ac_v):
                    e.wait_ge(s_ac, ac_v)
                    e.reciprocal(VARf, VARf)
                    e.tensor_copy(MUr, MUf)
                    e.tensor_copy(RSr, VARf).then_inc(s_dv, 1)
                c.sec("vector", dv2)
                c.inc("dv")
                rb = bcast_rows(pa, pb)

                def dvw(e, rb=rb):
                    e.wait_ge(s_pe, rb)
                    e.tensor_copy(SHR[0:1, 0:1], SHR[0:1, 0:1]).then_inc(s_dv, 1)
                c.sec("vector", dvw)
                c.inc("dv")
            c.barrier()

        def emit_mlp_select(pm, pr):
            with contextlib.ExitStack() as ps:
                XNH = ps.enter_context(nc.sbuf_tensor("XNH", [128, NF * 512], BF16))
                GBUF = ps.enter_context(nc.sbuf_tensor("GBUF", [128, 64 * 512], BF16))
                WS2 = ps.enter_context(nc.sbuf_tensor("WS2", [128, 2 * 4096], BF16))
                XI2 = ps.enter_context(nc.sbuf_tensor("XI2", [128, 2 * 512], BF16))
                HT2 = ps.enter_context(nc.sbuf_tensor("HT2", [128, 2 * 512], BF16))
                pf0 = ps.enter_context(nc.psum_tensor("pf0", [128, 512], F32))
                pf1 = ps.enter_context(nc.psum_tensor("pf1", [128, 512], F32))
                pd = ps.enter_context(nc.psum_tensor("pdl", [128, 512], F32))
                pfs = [pf0, pf1]

                for th in range(2):
                    with contextlib.ExitStack() as ps2:
                        XT = ps2.enter_context(nc.sbuf_tensor("XTa%d" % th,
                                                              [128, 2 * 512], BF16))
                        for f in range(NF):
                            par = f % 2

                            def ld(e, f=f, par=par, th=th, w=c.v["dv"] - 1):
                                if w > 0:
                                    e.wait_ge(s_dv, w)
                                e.dma_start(out=XT[:, par * 512:(par + 1) * 512],
                                            in_=xisp[:, f * TC + th * 512:
                                                     f * TC + th * 512 + 512]
                                            ).then_inc(s_dS, 16)
                            c.sec("sync", ld)
                            ld_v = c.inc("dS", 16)

                            def dv_a(e, f=f, par=par, th=th, ld_v=ld_v):
                                e.wait_ge(s_dS, ld_v)
                                e.tensor_sub(SHR[:, 0:512],
                                             XT[:, par * 512:(par + 1) * 512],
                                             pm[:, th * 512:th * 512 + 512])
                                e.tensor_mul(SHR[:, 512:1024], SHR[:, 0:512],
                                             pr[:, th * 512:th * 512 + 512])
                                e.tensor_scalar(XNH[:, f * 512:(f + 1) * 512],
                                                SHR[:, 512:1024],
                                                L2Sv[:, f:f + 1], L2Bv[:, f:f + 1],
                                                ALU.mult, ALU.add).then_inc(s_dv, 1)
                            c.sec("vector", dv_a)
                            c.inc("dv")
                    c.barrier()

                    for o in range(64):
                        par = o % 2

                        def wld(e, o=o, par=par, th=th, w=c.v["pe"]):
                            if o >= 2 or th > 0:
                                e.wait_ge(s_pe, w - 1)
                            e.dma_start(out=WS2[:, par * 4096:par * 4096 + 2048],
                                        in_=wf1p[o, :, :]).then_inc(s_dG, 16)
                        c.sec("gpsimd", wld)
                        w_v = c.inc("dG", 16)

                        def pe_fn(t, o=o, par=par, w_v=w_v, w=c.v["ac"] - 1,
                                  dvv=c.v["dv"]):
                            t.wait_ge(s_dG, w_v)
                            if o == 0:
                                t.wait_ge(s_dv, dvv)
                            if w > 0:
                                t.wait_ge(s_ac, w)
                            for f in range(NF):
                                ins = t.matmul(pfs[par][:, :],
                                               WS2[:, par * 4096 + f * 128:
                                                   par * 4096 + (f + 1) * 128],
                                               XNH[:, f * 512:(f + 1) * 512],
                                               start=(f == 0), stop=(f == NF - 1))
                            ins.then_inc(s_pe, 1)
                        c.sec("tensor", pe_fn)
                        pe_v = c.inc("pe")

                        def ac_g(e, o=o, par=par, pe_v=pe_v):
                            e.wait_ge(s_pe, pe_v)
                            e.activation(GBUF[:, o * 512:(o + 1) * 512],
                                         pfs[par][:, :],
                                         AF.Gelu_apprx_tanh).then_inc(s_ac, 1)
                        c.sec("scalar", ac_g)
                        c.inc("ac")
                    c.barrier()

                    for o in range(NF):
                        par = o % 2

                        def ld_xh(e, o=o, par=par, th=th, w=c.v["dv"] - 1):
                            if w > 0:
                                e.wait_ge(s_dv, w)
                            e.dma_start(out=XI2[:, par * 512:(par + 1) * 512],
                                        in_=xisp[:, o * TC + th * 512:
                                                 o * TC + th * 512 + 512]
                                        ).then_inc(s_dS, 16)
                            e.dma_start(out=HT2[:, par * 512:(par + 1) * 512],
                                        in_=hsp[:, o * TC + th * 512:
                                                o * TC + th * 512 + 512]
                                        ).then_inc(s_dS, 16)
                        c.sec("sync", ld_xh)
                        xh_v = c.inc("dS", 32)

                        w_vs = []
                        for q in range(2):
                            def wld(e, o=o, q=q, th=th, w=c.v["pe"]):
                                if o + th > 0:
                                    e.wait_ge(s_pe, w - (1 if q == 0 else 0))
                                e.dma_start(out=WS2[:, q * 4096:(q + 1) * 4096],
                                            in_=wf2p[o, :, q * 4096:(q + 1) * 4096]
                                            ).then_inc(s_dG, 16)
                            c.sec("gpsimd", wld)
                            w_vs.append(c.inc("dG", 16))

                        def pe_f2a(t, o=o, par=par, w0=w_vs[0], w=c.v["dv"] - 1):
                            if w > 0:
                                t.wait_ge(s_dv, w)
                            t.wait_ge(s_dG, w0)
                            for k in range(32):
                                ins = t.matmul(pfs[par][:, :],
                                               WS2[:, k * 128:(k + 1) * 128],
                                               GBUF[:, k * 512:(k + 1) * 512],
                                               start=(k == 0), stop=False)
                            ins.then_inc(s_pe, 1)
                        c.sec("tensor", pe_f2a)
                        c.inc("pe")

                        def pe_f2b(t, o=o, par=par, th=th, w1=w_vs[1],
                                   wsel=c.v["dv"]):
                            t.wait_ge(s_dG, w1)
                            for k in range(32, 64):
                                t.matmul(pfs[par][:, :],
                                         WS2[:, k * 128:(k + 1) * 128],
                                         GBUF[:, k * 512:(k + 1) * 512],
                                         start=False, stop=(k == 63))
                            if wsel > 0:
                                t.wait_ge(s_dv, wsel)
                            t.matmul(pd[:, :], WUPS[:, o * 128:(o + 1) * 128],
                                     ZLR[:, th * 512:th * 512 + 512],
                                     start=True, stop=True).then_inc(s_pe, 1)
                        c.sec("tensor", pe_f2b)
                        pe_v = c.inc("pe")

                        def dv_sel(e, o=o, th=th, par=par, pe_v=pe_v, xh_v=xh_v):
                            e.wait_ge(s_pe, pe_v)
                            e.wait_ge(s_dS, xh_v)
                            psm = pfs[par]
                            xi = XI2[:, par * 512:(par + 1) * 512]
                            ht = HT2[:, par * 512:(par + 1) * 512]
                            t0 = SHR[:, 0:512]
                            e.tensor_add(psm[:, :], psm[:, :], xi)
                            e.tensor_add(SHR[:, 512:1024], pd[:, :], ht)
                            e.select(psm[:, :],
                                     MSKB[:, TC + th * 512:TC + th * 512 + 512],
                                     SHR[:, 512:1024], psm[:, :])
                            e.tensor_scalar_add(t0, ht, SBVv[:, o:o + 1])
                            e.select(psm[:, :], MSKB[:, th * 512:th * 512 + 512],
                                     t0, psm[:, :])
                            e.scalar_tensor_tensor(
                                XIX[:, o * TC + th * 512:o * TC + th * 512 + 512],
                                psm[:, :], 0.5,
                                XIX[:, o * TC + th * 512:o * TC + th * 512 + 512],
                                ALU.mult, ALU.add).then_inc(s_dv, 1)
                        c.sec("vector", dv_sel)
                        c.inc("dv")
                    c.barrier()

        def emit_ffn(tag):
            with contextlib.ExitStack() as ps:
                HBF = ps.enter_context(nc.sbuf_tensor("HBF%d" % tag,
                                                      [128, NF * TC], BF16))
                emit_ln_mac(HBF)
                if upto == "h%d" % tag:
                    dump("dbg_h", HBF[:, :], BF16); finish(); return True
                emit_router(HBF)
                if upto == "rt%d" % tag:
                    dump("dbg_msk", MSKB[:, :], U8)
                    dump("dbg_z", ZLR[:, :], BF16); finish(); return True

                def hspill(e):
                    e.dma_start(out=hsp[:, :], in_=HBF[:, :]).then_inc(s_dA, 16)
                c.sec("scalar", hspill)
                c.inc("dA", 16)

                with contextlib.ExitStack() as ps2:
                    XN = ps2.enter_context(nc.sbuf_tensor("XN%d" % tag,
                                                          [128, NF * TC], BF16))
                    emit_ln_aff(lambda f: HBF[:, f * TC:(f + 1) * TC],
                                L1Sv, L1Bv,
                                lambda f: XN[:, f * TC:(f + 1) * TC])
                    if upto == "xn%d" % tag:
                        dump("dbg_xn", XN[:, :], BF16); finish(); return True
                    emit_gather(XN, agx_in, agx_out)
            with contextlib.ExitStack() as ps:
                KF = ps.enter_context(nc.sbuf_tensor("KF%d" % tag, [128, 4 * S], BF16))
                VT = ps.enter_context(nc.sbuf_tensor("VT%d" % tag, [128, 4 * S], BF16))
                emit_qkv(KF, VT)
                if upto == "qkv%d" % tag:
                    dump("dbg_k", KF[:, :], BF16)
                    dump("dbg_v", VT[:, :], BF16); finish(); return True
                emit_attn(KF, VT)
            if upto == "ctx%d" % tag:
                dump_dram("dbg_ctx", agc, 4 * S); finish(); return True
            emit_partial_rs(wophT, agc, rs_in, rs_out, "o%d" % tag)
            emit_xi()
            if upto == "xi%d" % tag:
                dump_dram("dbg_xi", xisp, NF * TC); finish(); return True
            with contextlib.ExitStack() as ps:
                pm = ps.enter_context(nc.psum_tensor("l2m", [128, TC], F32))
                pr = ps.enter_context(nc.psum_tensor("l2r", [128, TC], F32))
                emit_ln2_stats(pm, pr)
                emit_mlp_select(pm, pr)
            if upto == "ffn%d" % tag:
                dump("dbg_x", XIX[:, :], F32); finish(); return True
            return False

        def emit_mixer():
            with contextlib.ExitStack() as ps:
                HBF = ps.enter_context(nc.sbuf_tensor("HBFm", [128, NF * TC], BF16))
                emit_ln_mac(HBF)
                if upto == "hm":
                    dump("dbg_h", HBF[:, :], BF16); finish(); return True
                emit_gather(HBF, agm_in, agm_out)

            with contextlib.ExitStack() as ps:
                KPF = ps.enter_context(nc.sbuf_tensor("KPF", [128, 4 * S], BF16))
                VT = ps.enter_context(nc.sbuf_tensor("VTm", [128, 4 * S], BF16))
                LCOL = ps.enter_context(nc.sbuf_tensor("LCOL", [128, 128], F32))
                RLC = ps.enter_context(nc.sbuf_tensor("RLC", [128, 128], F32))
                RLB = ps.enter_context(nc.sbuf_tensor("RLB", [128, 128], BF16))

                with contextlib.ExitStack() as ps2:
                    WMS = ps2.enter_context(nc.sbuf_tensor("WMS", [128, 4 * D], BF16))
                    XG = ps2.enter_context(nc.sbuf_tensor("XGm", [128, NF * 512], BF16))
                    SQB = ps2.enter_context(nc.sbuf_tensor("SQB", [128, 512], BF16))
                    PWF = ps2.enter_context(nc.sbuf_tensor("PWF", [128, 512], F32))
                    RTMP = ps2.enter_context(nc.sbuf_tensor("RTMP", [128, 512], BF16))
                    VTMP = ps2.enter_context(nc.sbuf_tensor("VTMP", [128, 512], BF16))
                    pw = ps2.enter_context(nc.psum_tensor("pwm", [128, 512], F32))
                    psq = ps2.enter_context(nc.psum_tensor("psq", [1, 512], F32))
                    pBm = ps2.enter_context(nc.psum_tensor("pBm", [128, 512], F32))
                    ptm = ps2.enter_context(nc.psum_tensor("ptm", [128, 512], BF16))

                    def wld(e):
                        for hl in range(4):
                            for f in range(NF):
                                e.dma_start(out=WMS[:, hl * D + f * 128:
                                                   hl * D + (f + 1) * 128],
                                            in_=wmph[hl, f, :, :]).then_inc(s_dG, 16)
                    c.sec("gpsimd", wld)
                    w_v = c.inc("dG", 4 * NF * 16)

                    for tb in range(NT):
                        def xg(e, tb=tb, w=c.v["pe"]):
                            if tb > 0:
                                e.wait_ge(s_pe, w)
                            r0 = 128 * (tb // 2)
                            cs = (tb % 2) * 512
                            for f in range(NF):
                                e.dma_start(out=XG[:, f * 512:(f + 1) * 512],
                                            in_=agm_out[r0:r0 + 128,
                                                        f * TC + cs:f * TC + cs + 512]
                                            ).then_inc(s_dS, 16)
                        c.sec("sync", xg)
                        xg_v = c.inc("dS", NF * 16)

                        for hl in range(4):
                            def pe_w(t, tb=tb, hl=hl, xg_v=xg_v, w_v=w_v,
                                     w=c.v["dv"]):
                                t.wait_ge(s_dS, xg_v)
                                if tb == 0 and hl == 0:
                                    t.wait_ge(s_dG, w_v)
                                if w > 0:
                                    t.wait_ge(s_dv, w)
                                for f in range(NF):
                                    ins = t.matmul(pw[:, :],
                                                   WMS[:, hl * D + f * 128:
                                                       hl * D + (f + 1) * 128],
                                                   XG[:, f * 512:(f + 1) * 512],
                                                   start=(f == 0),
                                                   stop=(f == NF - 1))
                                ins.then_inc(s_pe, 1)
                            c.sec("tensor", pe_w)
                            pw_v = c.inc("pe")

                            def ac_sq(e, pw_v=pw_v):
                                e.wait_ge(s_pe, pw_v)
                                e.activation(SQB[:, :], pw[:, :], AF.Square)
                                e.activation(PWF[:, :], pw[:, :],
                                             AF.Copy).then_inc(s_ac, 1)
                            c.sec("scalar", ac_sq)
                            sq_v = c.inc("ac")

                            def pe_s(t, sq_v=sq_v):
                                t.wait_ge(s_ac, sq_v)
                                t.matmul(psq[0:1, :], ONESC, SQB[:, :],
                                         start=True, stop=True).then_inc(s_pe, 1)
                            c.sec("tensor", pe_s)
                            ps_v = c.inc("pe")

                            def ac_r(e, ps_v=ps_v):
                                e.wait_ge(s_pe, ps_v)
                                e.activation(SHR[0:1, 0:512], psq[0:1, :], AF.Sqrt,
                                             bias=EPSR[0:1, 0:1],
                                             scale=INV128[0:1, 0:1]).then_inc(s_ac, 1)
                            c.sec("scalar", ac_r)
                            ar_v = c.inc("ac")

                            def dv_r(e, ar_v=ar_v):
                                e.wait_ge(s_ac, ar_v)
                                e.reciprocal(SHR[0:1, 512:1024], SHR[0:1, 0:512])
                                e.tensor_copy(ROWSB[0:1, 0:512],
                                              SHR[0:1, 512:1024]).then_inc(s_dv, 1)
                            c.sec("vector", dv_r)
                            dr_v = c.inc("dv")

                            def pe_b(t, dr_v=dr_v):
                                t.wait_ge(s_dv, dr_v)
                                t.matmul(pBm[:, :], ONESR, ROWSB[0:1, 0:512],
                                         start=True, stop=True).then_inc(s_pe, 1)
                            c.sec("tensor", pe_b)
                            pb_v = c.inc("pe")

                            def dv_rkv(e, tb=tb, hl=hl, pb_v=pb_v,
                                       w=c.v["pe"] - 1):
                                e.wait_ge(s_pe, pb_v)
                                e.tensor_mul(RTMP[:, :], PWF[:, :], pBm[:, :])
                                e.tensor_scalar(KPF[:, hl * S + tb * 512:
                                                    hl * S + (tb + 1) * 512],
                                                RTMP[:, :], GG, None, ALU.mult)
                                e.tensor_scalar(VTMP[:, :], RTMP[:, :],
                                                G3, None, ALU.mult).then_inc(s_dv, 1)
                            c.sec("vector", dv_rkv)
                            rk_v = c.inc("dv")

                            def st_r(e, tb=tb, hl=hl, rk_v=rk_v):
                                e.wait_ge(s_dv, rk_v)
                                e.dma_start(out=agr[:, hl * S + tb * 512:
                                                   hl * S + (tb + 1) * 512],
                                            in_=RTMP[:, :]).then_inc(s_dA, 16)
                            c.sec("scalar", st_r)
                            sr_v = c.inc("dA", 16)

                            def pe_t(t, rk_v=rk_v, sr_v=sr_v):
                                t.wait_ge(s_dv, rk_v)
                                t.wait_ge(s_dA, sr_v)
                                for ts2 in range(4):
                                    ins = t.transpose(ptm[:, ts2 * 128:
                                                          (ts2 + 1) * 128],
                                                      VTMP[:, ts2 * 128:
                                                           (ts2 + 1) * 128],
                                                      ID_BF)
                                ins.then_inc(s_pe, 1)
                            c.sec("tensor", pe_t)
                            pt_v = c.inc("pe")

                            def dv_vt(e, tb=tb, hl=hl, pt_v=pt_v):
                                e.wait_ge(s_pe, pt_v)
                                for ts2 in range(4):
                                    e.tensor_copy(
                                        VT[:, (tb * 4 + ts2) * 512 + hl * 128:
                                           (tb * 4 + ts2) * 512 + (hl + 1) * 128],
                                        ptm[:, ts2 * 128:(ts2 + 1) * 128])
                                e.tensor_copy(SHR[0:1, 0:1],
                                              SHR[0:1, 0:1]).then_inc(s_dv, 1)
                            c.sec("vector", dv_vt)
                            c.inc("dv")
                c.barrier()
                if upto == "wrm":
                    dump("dbg_kp", KPF[:, :], BF16)
                    dump("dbg_vt", VT[:, :], BF16); finish(); return True

                with contextlib.ExitStack() as ps2:
                    RJ = ps2.enter_context(nc.sbuf_tensor("RJ1", [128, 2 * 128], BF16))
                    ETJ = ps2.enter_context(nc.sbuf_tensor("ETJ", [128, 2 * 512], BF16))
                    pE0 = ps2.enter_context(nc.psum_tensor("pm10", [128, 512], F32))
                    pE1 = ps2.enter_context(nc.psum_tensor("pm11", [128, 512], F32))
                    pEs = [pE0, pE1]

                    def z0(e):
                        e.memset(LCOL[:, :], 0.0).then_inc(s_dv, 1)
                    c.sec("vector", z0)
                    c.inc("dv")

                    for hl in range(4):
                        for js in range(32):
                            jpar = (hl * 32 + js) % 2

                            def rld(e, hl=hl, js=js, jpar=jpar, w=c.v["pe"]):
                                if hl + js > 0:
                                    e.wait_ge(s_pe, w)
                                e.dma_start(out=RJ[:, jpar * 128:(jpar + 1) * 128],
                                            in_=agr[:, hl * S + js * 128:
                                                   hl * S + (js + 1) * 128]
                                            ).then_inc(s_dS, 16)
                            c.sec("sync", rld)
                            r_v = c.inc("dS", 16)

                            for g in range(8):
                                par = g % 2

                                def pe_a(t, hl=hl, g=g, par=par, jpar=jpar,
                                         r_v=r_v, w=c.v["ac"] - 1):
                                    if g == 0:
                                        t.wait_ge(s_dS, r_v)
                                    if w > 0:
                                        t.wait_ge(s_ac, w)
                                    t.matmul(pEs[par][:, :],
                                             RJ[:, jpar * 128:(jpar + 1) * 128],
                                             KPF[:, hl * S + g * 512:
                                                 hl * S + (g + 1) * 512],
                                             start=True,
                                             stop=True).then_inc(s_pe, 1)
                                c.sec("tensor", pe_a)
                                a_v = c.inc("pe")

                                def ac_e(e, par=par, a_v=a_v, w=c.v["dv"] - 1):
                                    e.wait_ge(s_pe, a_v)
                                    if w > 0:
                                        e.wait_ge(s_dv, w)
                                    e.activation(ETJ[:, par * 512:(par + 1) * 512],
                                                 pEs[par][:, :], AF.Exp, bias=CMX,
                                                 accum_out=PST[:, par:par + 1]
                                                 ).then_inc(s_ac, 1)
                                c.sec("scalar", ac_e)
                                e_v = c.inc("ac")

                                LACC = VEC[:, 100:102]

                                def dv_l(e, hl=hl, js=js, g=g, par=par, e_v=e_v):
                                    e.wait_ge(s_ac, e_v)
                                    col = LACC[:, par:par + 1]
                                    if g < 2:
                                        e.tensor_copy(col, PST[:, par:par + 1]
                                                      ).then_inc(s_dv, 1)
                                    elif g < 7:
                                        e.tensor_add(col, col, PST[:, par:par + 1]
                                                     ).then_inc(s_dv, 1)
                                    else:
                                        e.tensor_add(col, col, PST[:, par:par + 1])
                                        e.tensor_copy(SHR[0:1, 0:1], VEC[0:1, 0:1])
                                        e.tensor_add(
                                            LCOL[:, hl * 32 + js:hl * 32 + js + 1],
                                            LACC[:, 0:1], LACC[:, 1:2]
                                        ).then_inc(s_dv, 1)
                                c.sec("vector", dv_l)
                                c.inc("dv")

                    def dv_rl(e, w=c.v["dv"]):
                        e.tensor_copy(SHR[0:1, 0:1], VEC[0:1, 0:1])
                        e.reciprocal(RLC[:, :], LCOL[:, :])
                        e.tensor_copy(SHR[0:1, 1:2], VEC[0:1, 0:1])
                        e.tensor_copy(RLB[:, :], RLC[:, :]).then_inc(s_dv, 1)
                    c.sec("vector", dv_rl)
                    c.inc("dv")
                c.barrier()
                if upto == "mp1":
                    dump("dbg_lc", LCOL[:, :], F32); finish(); return True

                with contextlib.ExitStack() as ps2:
                    RJ5 = ps2.enter_context(nc.sbuf_tensor("RJ5", [128, 2 * 512], BF16))
                    ET = ps2.enter_context(nc.sbuf_tensor("ETm", [128, 2 * 512], BF16))
                    VTL = ps2.enter_context(nc.sbuf_tensor("VTL", [128, 2 * 128], BF16))
                    YST = ps2.enter_context(nc.sbuf_tensor("YST", [128, 2 * 512], BF16))
                    pE0 = ps2.enter_context(nc.psum_tensor("pm20", [128, 512], F32))
                    pE1 = ps2.enter_context(nc.psum_tensor("pm21", [128, 512], F32))
                    py1 = ps2.enter_context(nc.psum_tensor("py1", [128, 512], F32))
                    py2 = ps2.enter_context(nc.psum_tensor("py2", [128, 512], F32))
                    ptl = ps2.enter_context(nc.psum_tensor("ptl", [4, 128], BF16))
                    pBy = ps2.enter_context(nc.psum_tensor("pBy", [128, 512], F32))
                    pEs = [pE0, pE1]

                    for hl in range(4):
                        for j in range(NT):
                            jpar = (hl * NT + j) % 2

                            def rld(e, hl=hl, j=j, jpar=jpar, w=c.v["pe"]):
                                if hl + j > 0:
                                    e.wait_ge(s_pe, w)
                                e.dma_start(out=RJ5[:, jpar * 512:(jpar + 1) * 512],
                                            in_=agr[:, hl * S + j * 512:
                                                   hl * S + (j + 1) * 512]
                                            ).then_inc(s_dS, 16)
                            c.sec("sync", rld)
                            r_v = c.inc("dS", 16)

                            for si in range(32):
                                par = si % 2

                                def pe_a(t, hl=hl, si=si, par=par, jpar=jpar,
                                         r_v=r_v, w=c.v["ac"] - 1):
                                    if si == 0:
                                        t.wait_ge(s_dS, r_v)
                                    if w > 0:
                                        t.wait_ge(s_ac, w)
                                    t.matmul(pEs[par][:, :],
                                             KPF[:, hl * S + si * 128:
                                                 hl * S + (si + 1) * 128],
                                             RJ5[:, jpar * 512:(jpar + 1) * 512],
                                             start=True,
                                             stop=True).then_inc(s_pe, 1)
                                c.sec("tensor", pe_a)
                                a_v = c.inc("pe")

                                def ac_e(e, par=par, a_v=a_v, w=c.v["pe"] - 2):
                                    e.wait_ge(s_pe, a_v)
                                    if w > 0:
                                        e.wait_ge(s_pe, w)
                                    e.activation(ET[:, par * 512:(par + 1) * 512],
                                                 pEs[par][:, :], AF.Exp,
                                                 bias=CMX).then_inc(s_ac, 1)
                                c.sec("scalar", ac_e)
                                e_v = c.inc("ac")

                                def dv_vl(e, hl=hl, si=si, par=par,
                                          w=c.v["pe"] - 2):
                                    if w > 0:
                                        e.wait_ge(s_pe, w)
                                    e.tensor_scalar(VTL[:, par * 128:(par + 1) * 128],
                                                    VT[:, si * 512 + hl * 128:
                                                       si * 512 + (hl + 1) * 128],
                                                    RLC[:, hl * 32 + si:
                                                        hl * 32 + si + 1],
                                                    None, ALU.mult).then_inc(s_dv, 1)
                                c.sec("vector", dv_vl)
                                v_v = c.inc("dv")

                                def pe_y(t, hl=hl, si=si, par=par, e_v=e_v,
                                         v_v=v_v, w0=c.v["dv"] - 1):
                                    t.wait_ge(s_ac, e_v)
                                    t.wait_ge(s_dv, v_v)
                                    if si == 0 and w0 > 0:
                                        t.wait_ge(s_dv, w0)
                                    t.matmul(py1[:, :],
                                             VT[:, si * 512 + hl * 128:
                                                si * 512 + (hl + 1) * 128],
                                             ET[:, par * 512:(par + 1) * 512],
                                             start=(si == 0), stop=(si == 31))
                                    t.matmul(py2[:, :],
                                             VTL[:, par * 128:(par + 1) * 128],
                                             ET[:, par * 512:(par + 1) * 512],
                                             start=(si == 0),
                                             stop=(si == 31)).then_inc(s_pe, 1)
                                c.sec("tensor", pe_y)
                                c.inc("pe")

                            def pe_tl(t, hl=hl, j=j, w=c.v["pe"]):
                                t.transpose(ptl[0:4, 0:128],
                                            RLB[:, hl * 32 + j * 4:
                                                hl * 32 + j * 4 + 4],
                                            ID_BF).then_inc(s_pe, 1)
                            c.sec("tensor", pe_tl)
                            tl_v = c.inc("pe")

                            def dv_tl(e, tl_v=tl_v):
                                e.wait_ge(s_pe, tl_v)
                                e.tensor_copy(SM4[0:4, 0:128],
                                              ptl[0:4, 0:128]).then_inc(s_dv, 1)
                            c.sec("vector", dv_tl)
                            tl2_v = c.inc("dv")

                            def mv_tl(e, tl2_v=tl2_v):
                                e.wait_ge(s_dv, tl2_v)
                                for r2 in range(4):
                                    e.dma_start(out=ROWSB[0:1, r2 * 128:(r2 + 1) * 128],
                                                in_=SM4[r2:r2 + 1, 0:128]
                                                ).then_inc(s_dS, 16)
                            c.sec("sync", mv_tl)
                            mv_v = c.inc("dS", 4 * 16)

                            def pe_by(t, mv_v=mv_v):
                                t.wait_ge(s_dS, mv_v)
                                t.matmul(pBy[:, :], ONESR, ROWSB[0:1, 0:512],
                                         start=True, stop=True).then_inc(s_pe, 1)
                            c.sec("tensor", pe_by)
                            by_v = c.inc("pe")

                            def dv_y(e, jpar=jpar, by_v=by_v, w=c.v["dA"] - 16):
                                e.wait_ge(s_pe, by_v)
                                if w > 0:
                                    e.wait_ge(s_dA, w)
                                e.tensor_copy(SHR[:, 512:1024], py1[:, :])
                                e.tensor_mul(SHR[:, 0:512], SHR[:, 512:1024],
                                             pBy[:, :])
                                e.tensor_add(YST[:, jpar * 512:(jpar + 1) * 512],
                                             SHR[:, 0:512], py2[:, :]).then_inc(s_dv, 1)
                            c.sec("vector", dv_y)
                            y_v = c.inc("dv")

                            def st_y(e, hl=hl, j=j, jpar=jpar, y_v=y_v):
                                e.wait_ge(s_dv, y_v)
                                e.dma_start(out=agy[:, hl * S + j * 512:
                                                   hl * S + (j + 1) * 512],
                                            in_=YST[:, jpar * 512:(jpar + 1) * 512]
                                            ).then_inc(s_dA, 16)
                            c.sec("scalar", st_y)
                            c.inc("dA", 16)
                c.barrier()
            if upto == "mp2":
                dump_dram("dbg_y", agy, 4 * S); finish(); return True

            emit_partial_rs(wWp, agy, rs2_in, rs2_out, "m")

            with contextlib.ExitStack() as ps:
                XT = ps.enter_context(nc.sbuf_tensor("XTm2", [128, 2 * TC], F32))
                for f in range(NF):
                    par = f % 2

                    def ld(e, f=f, par=par, w=c.v["dv"] - 1):
                        if w > 0:
                            e.wait_ge(s_dv, w)
                        e.dma_start(out=XT[:, par * TC:(par + 1) * TC],
                                    in_=rs2_out[f * 128:(f + 1) * 128, :]
                                    ).then_inc(s_dS, 16)
                    c.sec("sync", ld)
                    ld_v = c.inc("dS", 16)

                    def dv_a(e, f=f, par=par, ld_v=ld_v):
                        e.wait_ge(s_dS, ld_v)
                        e.tensor_add(XIX[:, f * TC:(f + 1) * TC],
                                     XIX[:, f * TC:(f + 1) * TC],
                                     XT[:, par * TC:(par + 1) * TC]).then_inc(s_dv, 1)
                    c.sec("vector", dv_a)
                    c.inc("dv")
            c.barrier()
            if upto == "mix":
                dump("dbg_x", XIX[:, :], F32); finish(); return True
            return False

        if upto == "p0":
            dump("dbg_v", VEC[:, 0:16], F32)
            dump("dbg_c", CONS[:, :], BF16)
            finish()
            return nc
        if emit_ffn(0):
            return nc
        if emit_mixer():
            return nc
        if emit_ffn(1):
            return nc
        finish()
    return nc


def prep(inputs):
    ac = np.ascontiguousarray
    x = np.asarray(inputs["L0_attn_x"], dtype=np.float32)
    W = np.asarray(inputs["L0_attn_W"], np.float32)
    wq = np.asarray(inputs["L0_ffn_wq"], np.float32)
    wk = np.asarray(inputs["L0_ffn_wk"], np.float32)
    wv = np.asarray(inputs["L0_ffn_wv"], np.float32)
    wo = np.asarray(inputs["L0_ffn_wo"], np.float32)

    ones = np.zeros((128, 264), dtype=ml_dtypes.bfloat16)
    ones[:, 0:128] = np.eye(128)
    ones[:, 128] = 1
    ones[0, 136:264] = 1
    eps = np.zeros((128, 8), np.float32)
    eps[:, 0] = EPS_LN; eps[:, 1] = EPS_RMS; eps[:, 2] = 1.0 / 128
    eps[:, 3] = 1.0 / math.sqrt(DH); eps[:, 4] = -C_FFN; eps[:, 5] = -C_MIX
    dmsk = np.zeros((128, 4 * 512), dtype=ml_dtypes.bfloat16)
    for s in range(4):
        p = np.arange(128)[:, None]
        j = np.arange(512)[None, :]
        dmsk[:, s * 512:(s + 1) * 512] = (s * 128 + p <= j).astype(ml_dtypes.bfloat16)

    shared = {
        "wf1p": pack_lhsT(ac(np.asarray(inputs["L0_ffn_w_ff1"], np.float32).T)),
        "wf2p": pack_lhsT(ac(np.asarray(inputs["L0_ffn_w_ff2"], np.float32).T)),
        "wdnp": ac(np.asarray(inputs["L0_ffn_w_down"], np.float32).T
                   .reshape(NF, 128, R).transpose(1, 0, 2).reshape(128, NF * R)),
        "wupp": ac(np.asarray(inputs["L0_ffn_w_up"], np.float32).T),
        "wrtp": ac(np.asarray(inputs["L0_ffn_w_router"], np.float32).T
                   .reshape(NF, 128, 3).transpose(1, 0, 2).reshape(128, NF * 3)),
        "g1v": np.asarray(inputs["L0_attn_g1"], np.float32).reshape(128, 1),
        "g2v": np.asarray(inputs["L0_attn_g2"], np.float32).reshape(128, 1),
        "g3v": np.asarray(inputs["L0_attn_g3"], np.float32).reshape(128, 1),
        "sbv": pack_vec(inputs["L0_ffn_simple_bias"]),
        "l1s": pack_vec(inputs["L0_ffn_ln1_scale"]),
        "l1b": pack_vec(inputs["L0_ffn_ln1_bias"]),
        "l2s": pack_vec(inputs["L0_ffn_ln2_scale"]),
        "l2b": pack_vec(inputs["L0_ffn_ln2_bias"]),
        "idbf": np.eye(128, dtype=ml_dtypes.bfloat16),
        "onescr": ones,
        "epscst": eps,
        "dmskin": dmsk,
    }

    def headpack(wm, h0):
        out = np.empty((4, NF, 128, 128), np.float32)
        for hl in range(4):
            sl = wm[(h0 + hl) * 128:(h0 + hl + 1) * 128, :].T
            out[hl] = sl.reshape(NF, 128, 128)
        return ac(out)

    def colpack(wm, h0):
        out = np.empty((4, NF, 128, 128), np.float32)
        for hl in range(4):
            sl = wm[(h0 + hl) * 128:(h0 + hl + 1) * 128, :]
            out[hl] = sl.reshape(128, NF, 128).transpose(1, 0, 2)
        return ac(out)

    in_maps = []
    for core in range(8):
        b, tau = core // 4, core % 4
        h0 = 4 * tau
        xt = x[b, tau * TC:(tau + 1) * TC, :].T
        xp = ac(xt.reshape(NF, 128, TC).transpose(1, 0, 2).reshape(128, NF * TC))
        wvT = wv[h0 * 128:(h0 + 4) * 128, :].T
        m = dict(shared)
        m["xpack"] = xp
        m["wqph"] = headpack(wq, h0)
        m["wkph"] = headpack(wk, h0)
        m["wvph"] = ac(wvT.reshape(NF, 128, 512))
        m["wmph"] = headpack(W, h0)
        m["wophT"] = colpack(ac(wo.T), h0)
        m["wWp"] = colpack(W, h0)
        in_maps.append(m)
    return in_maps


def unpack_out(res, key="xout"):
    out = np.empty((B, S, D), dtype=np.float32)
    for core in range(8):
        b, tau = core // 4, core % 4
        xp = np.asarray(res[core][key], np.float32)
        xt = xp.reshape(128, NF, TC).transpose(1, 0, 2).reshape(D, TC)
        out[b, tau * TC:(tau + 1) * TC, :] = xt.T
    return out


def kernel(**inputs):
    nc = build("full")
    in_maps = prep(inputs)
    res = run_bass_kernel_spmd(nc, in_maps, core_ids=list(range(8)))
    return unpack_out(res.results)
